# version 07
# baseline (speedup 1.0000x reference)
import numpy as np
import ml_dtypes
from contextlib import ExitStack

import concourse.bass as bass
import concourse.mybir as mybir
from concourse.bass_utils import run_bass_kernel_spmd
from concourse.bass_core_rust import IndirectOffsetOnAxis

F32 = mybir.dt.float32
BF16 = mybir.dt.bfloat16
I32 = mybir.dt.int32
AF = mybir.ActivationFunctionType
ALU = mybir.AluOpType
AX = mybir.AxisListType

B, T, D, HS, E = 32, 2048, 1024, 128, 8
NCORES = 8
SEQ = B // NCORES
NT = T // 128
KC = D // 128
ATTN_SCALE = float(HS) ** -0.5
CAP = 384
NG = E * CAP // 128
EXP_SHIFT = 16.0
LN_EPS = 1e-5

PHASES = ("ffn1", "attn", "ffn2")
ATTN_PIPE = True


class Sem:
    __slots__ = ("h", "n")

    def __init__(self, h):
        self.h = h
        self.n = 0


class KB:

    ENGINES = ("tensor", "vector", "scalar", "gpsimd", "sync")

    def __init__(self):
        self.q = {e: [] for e in self.ENGINES}
        self.waited = {e: {} for e in self.ENGINES}

    def emit(self, eng, fn, waits=(), inc=None, k=1):
        real = []
        for w in waits:
            if w is None:
                continue
            s, t = w
            if t is None:
                continue
            cur = self.waited[eng].get(id(s), -1)
            if t > cur:
                real.append((s.h, t))
                self.waited[eng][id(s)] = t
        tok = None
        ih = None
        if inc is not None:
            inc.n += k
            tok = inc.n
            ih = inc.h

        def run(e, real=real, fn=fn, ih=ih, k=k):
            for h, t in real:
                e.wait_ge(h, t)
            ins = fn(e)
            if ih is not None:
                ins.then_inc(ih, k)

        self.q[eng].append(run)
        return tok

    def barrier_wait(self, eng, waits):
        self.emit(eng, lambda e: None, waits=waits, inc=None)


def build_program():
    nc = bass.Bass("TRN2", target_bir_lowering=False)

    x_in = nc.dram_tensor("x_in", [SEQ, T, D], F32, kind="ExternalInput").ap()
    wq = nc.dram_tensor("wq", [2, HS, D], F32, kind="ExternalInput").ap()
    wk = nc.dram_tensor("wk", [HS, D], F32, kind="ExternalInput").ap()
    wv = nc.dram_tensor("wv", [HS, D], F32, kind="ExternalInput").ap()
    wo = nc.dram_tensor("wo", [D, 2 * HS], F32, kind="ExternalInput").ap()
    wr = nc.dram_tensor("wr", [D, E], F32, kind="ExternalInput").ap()
    rb = nc.dram_tensor("rb", [1, E], F32, kind="ExternalInput").ap()
    ew = nc.dram_tensor("ew", [E, D, D], F32, kind="ExternalInput").ap()
    eb = nc.dram_tensor("eb", [E, D], F32, kind="ExternalInput").ap()
    c_identf = nc.dram_tensor("c_identf", [128, 128], F32, kind="ExternalInput").ap()
    c_identb = nc.dram_tensor("c_identb", [128, 128], BF16, kind="ExternalInput").ap()
    c_tril = nc.dram_tensor("c_tril", [128, 128], BF16, kind="ExternalInput").ap()
    c_cos = nc.dram_tensor("c_cos", [128, NT * 64], BF16, kind="ExternalInput").ap()
    c_sin = nc.dram_tensor("c_sin", [128, NT * 64], BF16, kind="ExternalInput").ap()
    c_ones_col = nc.dram_tensor("c_ones_col", [128, 1], BF16, kind="ExternalInput").ap()
    c_ones_row = nc.dram_tensor("c_ones_row", [1, 128], F32, kind="ExternalInput").ap()
    c_ones_row_bf = nc.dram_tensor("c_ones_row_bf", [1, 128], BF16, kind="ExternalInput").ap()
    c_trilS = nc.dram_tensor("c_trilS", [128, 128], BF16, kind="ExternalInput").ap()
    c_iota8 = nc.dram_tensor("c_iota8", [128, E], F32, kind="ExternalInput").ap()
    c_base = nc.dram_tensor("c_base", [1, E], F32, kind="ExternalInput").ap()
    y_out = nc.dram_tensor("y", [SEQ, T, D], F32, kind="ExternalOutput").ap()
    g_x = nc.dram_tensor("g_x", [E * CAP + 8, D], BF16).ap()
    g_y = nc.dram_tensor("g_y", [E * CAP + 8, D], BF16).ap()
    ew_bf = nc.dram_tensor("ew_bf", [E, D, D], BF16).ap()
    eb_bfd = nc.dram_tensor("eb_bfd", [E, D], BF16).ap()

    kb = KB()

    with ExitStack() as es:
        sbuf = lambda n, s, d: es.enter_context(nc.sbuf_tensor(n, s, d))
        psum = lambda n, s, d=F32: es.enter_context(nc.psum_tensor(n, s, d))
        msem = lambda n: Sem(es.enter_context(nc.semaphore(n)))

        x_res = sbuf("x_res", [128, NT * D], F32)
        w_bufs = [sbuf(f"w_buf{i}", [128, KC * D], BF16) for i in range(2)]
        K_hs = sbuf("K_hs", [128, T], BF16)
        V_tok = sbuf("V_tok", [128, NT * HS], BF16)
        qkvT = sbuf("qkvT", [128, KC * 512], BF16)
        woT = sbuf("woT", [128, 2 * D], BF16)
        wr_bf = sbuf("wr_bf", [128, KC * E], BF16)
        rb_bf = sbuf("rb_bf", [1, E], BF16)
        cos_sb = sbuf("cos_sb", [128, NT * 64], BF16)
        sin_sb = sbuf("sin_sb", [128, NT * 64], BF16)
        identf_sb = sbuf("identf_sb", [128, 128], F32)
        identb_sb = sbuf("identb_sb", [128, 128], BF16)
        tril_sb = sbuf("tril_sb", [128, 128], BF16)
        onescol_sb = sbuf("onescol_sb", [128, 1], BF16)
        onesrow_sb = sbuf("onesrow_sb", [1, 128], F32)
        onesrow_bf = sbuf("onesrow_bf", [1, 128], BF16)
        trilS_sb = sbuf("trilS_sb", [128, 128], BF16)
        iota8_sb = sbuf("iota8_sb", [128, E], F32)
        base_sb = sbuf("base_sb", [1, E], F32)
        s_wraw = sbuf("s_wraw", [128, 2048], BF16)
        s_xn = [sbuf(f"s_xn{i}", [128, D], F32) for i in range(2)]
        s_xnT_f = sbuf("s_xnT_f", [128, D], F32)
        s_bn6 = [sbuf(f"s_bn6_{i}", [128, 12], F32) for i in range(2)]
        s_mv = [sbuf(f"s_mv{i}", [128, 2], F32) for i in range(2)]
        s_lnv = [sbuf(f"s_lnv{i}", [128, 1], F32) for i in range(2)]
        s_rstd = [sbuf(f"s_rstd{i}", [128, 1], F32) for i in range(2)]
        s_nmr = [sbuf(f"s_nmr{i}", [128, 1], F32) for i in range(2)]
        s_lmaxn = [sbuf(f"s_lmaxn{i}", [128, 1], F32) for i in range(2)]
        s_S = [sbuf(f"s_S{i}", [128, 1], F32) for i in range(2)]
        s_g = [sbuf(f"s_g{i}", [128, 1], F32) for i in range(2)]
        s_tmp8 = [sbuf(f"s_tmp8_{i}", [128, E], F32) for i in range(2)]
        s_pe8 = [sbuf(f"s_pe8_{i}", [128, E], F32) for i in range(2)]
        s_t8a = [sbuf(f"s_t8a{i}", [128, E], F32) for i in range(2)]
        s_t8b = [sbuf(f"s_t8b{i}", [128, E], F32) for i in range(2)]
        s_idx = [sbuf(f"s_idx{i}", [128, 1], F32) for i in range(2)]
        s_u = [sbuf(f"s_u{i}", [128, 1], F32) for i in range(2)]
        s_o = [sbuf(f"s_o{i}", [128, 1], F32) for i in range(2)]
        s_slotf = [sbuf(f"s_slotf{i}", [128, 1], F32) for i in range(2)]
        s_indbf = [sbuf(f"s_indbf{i}", [128, E], BF16) for i in range(2)]
        s_xnsc = [sbuf(f"s_xnsc{i}", [128, D], BF16) for i in range(2)]
        s_gx = [sbuf(f"s_gx{i}", [128, D], BF16) for i in range(2)]
        s_gxT = [sbuf(f"s_gxT{i}", [128, D], BF16) for i in range(2)]
        s_ygr = [sbuf(f"s_ygr{i}", [128, D], BF16) for i in range(2)]
        s_yg = [sbuf(f"s_yg{i}", [128, D], BF16) for i in range(2)]
        s_bgt = [sbuf(f"s_bgt{i}", [128, D], BF16) for i in range(2)]
        s_cnt = sbuf("s_cnt", [1, E], F32)
        s_offbase = sbuf("s_offbase", [1, E], F32)
        s_hg_all = sbuf("s_hg_all", [128, NT], F32)
        s_sloti_all = sbuf("s_sloti_all", [128, NT], I32)
        s_idxi_all = sbuf("s_idxi_all", [128, NT], I32)
        s_negC = sbuf("s_negC", [128, 1], F32)
        s_eps = sbuf("s_eps", [128, 1], F32)
        s_lnT_bf = [sbuf(f"s_lnT_bf{i}", [128, D], BF16) for i in range(2)]
        s_xnbf = [sbuf(f"s_xnbf{i}", [128, D], BF16) for i in range(2)]
        s_qk = [sbuf(f"s_qk{i}", [128, 3 * HS], BF16) for i in range(2)]
        s_qkr = [sbuf(f"s_qkr{i}", [128, 3 * HS], BF16) for i in range(2)]
        s_rt = [[sbuf(f"s_rt{i}_{j}", [128, 64], BF16) for j in range(2)]
                for i in range(4)]
        s_qT = [sbuf(f"s_qT{i}", [128, 2 * HS], BF16) for i in range(2)]
        s_P = sbuf("s_P", [128, 2 * 256], BF16)
        s_oT = [sbuf(f"s_oT{i}", [128, 256], BF16) for i in range(2)]
        s_lrow = [sbuf(f"s_lrow{i}", [1, 256], F32) for i in range(2)]
        s_rinv = [sbuf(f"s_rinv{i}", [1, 256], F32) for i in range(2)]

        PF = [psum(f"PF{i}", [128, 512], F32) for i in range(6)]
        PB = [psum(f"PB{i}", [128, 1024], BF16) for i in range(2)]

        sd_c = msem("sd_c")
        sd_x = msem("sd_x")
        sd_st = msem("sd_st")
        sd_ws = msem("sd_ws")
        sd_w0 = msem("sd_w0")
        sd_w1 = msem("sd_w1")
        s_pe = msem("s_pe")
        s_dve = msem("s_dve")
        s_act = msem("s_act")
        s_pool = msem("s_pool")
        sd_wbuf = [sd_w0, sd_w1]
        sd_wc = msem("sd_wc")
        sd_sc = [msem("sd_sc0"), msem("sd_sc1")]
        sd_gx = [msem("sd_gx0"), msem("sd_gx1")]
        sd_gy = [msem("sd_gy0"), msem("sd_gy1")]
        sd_yg = [msem("sd_yg0"), msem("sd_yg1")]
        sd_bg = [msem("sd_bg0"), msem("sd_bg1")]

        EMIT = kb.emit

        def dma_on(sem, dst, src, waits=()):
            EMIT("sync", lambda e, d=dst, s=src: e.dma_start(d, s),
                 waits=waits, inc=sem, k=16)
            return sem.n

        def dma_pool_on(sem, dst, src, waits=()):
            EMIT("gpsimd", lambda e, d=dst, s=src: e.dma_start(d, s),
                 waits=waits, inc=sem, k=16)
            return sem.n

        for dst, src in [
            (identf_sb[:], c_identf[:]), (identb_sb[:], c_identb[:]),
            (tril_sb[:], c_tril[:]), (cos_sb[:], c_cos[:]),
            (sin_sb[:], c_sin[:]), (onescol_sb[:], c_ones_col[:]),
            (onesrow_sb[:], c_ones_row[:]), (onesrow_bf[:], c_ones_row_bf[:]),
            (trilS_sb[:], c_trilS[:]), (iota8_sb[:], c_iota8[:]),
            (base_sb[:], c_base[:]),
        ]:
            dma_on(sd_c, dst, src)
        tok_consts = sd_c.n

        for t in range(NT):
            dma_on(sd_x, x_res[:, t * D:(t + 1) * D],
                   x_in[0, t * 128:(t + 1) * 128, :])
        tok_xload_all = sd_x.n

        tok_negC = EMIT("vector", lambda e: e.memset(s_negC[:], -EXP_SHIFT),
                        inc=s_dve)
        tok_eps = EMIT("vector", lambda e: e.memset(s_eps[:], LN_EPS), inc=s_dve)

        tok_zrow = EMIT("vector", lambda e: e.memset(s_yg[0][0:1, :], 0.0),
                        inc=s_dve)
        EMIT("sync", lambda e: e.dma_start(g_y[E * CAP:E * CAP + 1, :],
                                           s_yg[0][0:1, :]),
             waits=[(s_dve, tok_zrow)], inc=sd_c, k=16)
        tok_consts = sd_c.n

        dma_pool_on(sd_wc, ew_bf[:], ew[:])
        dma_pool_on(sd_wc, eb_bfd[:], eb[:])
        dma_pool_on(sd_wc, wr_bf[:].rearrange("p (kc e) -> p kc e", kc=KC),
                    wr.rearrange("(kc p) e -> p kc e", p=128))
        dma_pool_on(sd_wc, rb_bf[:], rb[:])
        tok_wcast = sd_wc.n

        def setup_transposes(src_tok, n_chunks, src_slice_fn, dst_fn, prev_evac):
            evac_toks = list(prev_evac)
            for c in range(n_chunks):
                bank = PB[c % 2]
                w = [(sd_ws, src_tok), (sd_c, tok_consts)]
                if len(evac_toks) >= 2:
                    w.append((s_act, evac_toks[-2]))
                tt = EMIT("tensor",
                          lambda e, c=c, bank=bank: e.transpose(
                              bank[:, 0:128], src_slice_fn(c), identb_sb[:]),
                          waits=w, inc=s_pe)
                ev = EMIT("scalar",
                          lambda e, c=c, bank=bank: e.copy(dst_fn(c), bank[:, 0:128]),
                          waits=[(s_pe, tt)], inc=s_act)
                evac_toks.append(ev)
            return evac_toks

        evt = []
        for h in range(2):
            dma_pool_on(sd_ws, s_wraw[:, h * D:(h + 1) * D], wq[h])
        tk = sd_ws.n
        for h in range(2):
            evt = setup_transposes(
                tk, KC,
                (lambda c, h=h: s_wraw[:, h * D + c * 128: h * D + (c + 1) * 128]),
                (lambda c, h=h: qkvT[:, c * 512 + h * 128: c * 512 + (h + 1) * 128]),
                evt)
        tk_k = EMIT("gpsimd", lambda e: e.dma_start(s_wraw[:, 0:D], wk[:]),
                    waits=[(s_act, evt[-1])], inc=sd_ws, k=16)
        tk_k = sd_ws.n
        evt = setup_transposes(
            tk_k, KC,
            (lambda c: s_wraw[:, c * 128:(c + 1) * 128]),
            (lambda c: qkvT[:, c * 512 + 256: c * 512 + 384]), evt)
        tk_v = EMIT("gpsimd", lambda e: e.dma_start(s_wraw[:, D:2 * D], wv[:]),
                    waits=[(s_act, evt[-1])], inc=sd_ws, k=16)
        tk_v = sd_ws.n
        evt = setup_transposes(
            tk_v, KC,
            (lambda c: s_wraw[:, D + c * 128: D + (c + 1) * 128]),
            (lambda c: qkvT[:, c * 512 + 384: c * 512 + 512]), evt)
        tk_o = EMIT("gpsimd",
                    lambda e: e.dma_start(
                        s_wraw[:].rearrange("p (rc j) -> p rc j", rc=8),
                        wo.rearrange("(rc p) j -> p rc j", p=128)),
                    waits=[(s_act, evt[-1])], inc=sd_ws, k=16)
        tk_o = sd_ws.n
        for h in range(2):
            evt = setup_transposes(
                tk_o, KC,
                (lambda c, h=h: s_wraw[:, c * 256 + h * 128: c * 256 + (h + 1) * 128]),
                (lambda c, h=h: woT[:, h * D + c * 128: h * D + (c + 1) * 128]),
                evt)
        tok_setup_tr = evt[-1]
        tok_setup_w = sd_ws.n

        def stream_expert(e_idx, buf_idx, wait_tok_pe):
            sem = sd_wbuf[buf_idx]
            for kc in range(KC):
                w = []
                if kc == 0:
                    w.append((sd_wc, tok_wcast))
                    if wait_tok_pe:
                        w.append((s_pe, wait_tok_pe))
                EMIT("gpsimd",
                     lambda e, e_idx=e_idx, kc=kc, buf_idx=buf_idx: e.dma_start(
                         w_bufs[buf_idx][:, kc * D:(kc + 1) * D],
                         ew_bf[e_idx, kc * 128:(kc + 1) * 128, :]),
                     waits=w, inc=sem, k=16)
            return sem.n

        last_y_evac = [None, None]
        tok_wbuf = [None, None]
        if "ffn1" in PHASES or "ffn2" in PHASES:
            tok_wbuf[0] = stream_expert(0, 0, None)
            tok_wbuf[1] = stream_expert(1, 1, None)
        last_mm_on_buf = [None, None]

        def emit_ln(x_slice, waits_x, par):
            bn6, mv, lnv, rstd, nmr = (s_bn6[par], s_mv[par], s_lnv[par],
                                       s_rstd[par], s_nmr[par])
            tA = EMIT("vector", lambda e: e.bn_stats(bn6[:, 0:6], x_slice[:, 0:512]),
                      waits=waits_x, inc=s_dve)
            tB = EMIT("vector", lambda e: e.bn_stats(bn6[:, 6:12], x_slice[:, 512:1024]),
                      inc=s_dve)
            tC = EMIT("vector", lambda e: e.bn_aggr(mv[:], bn6[:]),
                      waits=[(s_dve, tB)], inc=s_dve)
            tD = EMIT("scalar", lambda e: e.activation(
                lnv[:], mv[:, 1:2], AF.Ln, bias=s_eps[:, 0:1]),
                waits=[(s_dve, tC), (s_dve, tok_eps)], inc=s_act)
            tE = EMIT("scalar", lambda e: e.activation(
                rstd[:], lnv[:], AF.Exp, scale=-0.5),
                waits=[(s_act, tD)], inc=s_act)
            tF = EMIT("vector", lambda e: e.tensor_scalar(
                nmr[:], mv[:, 0:1], rstd[:, 0:1], -1.0, ALU.mult, ALU.mult),
                waits=[(s_act, tE)], inc=s_dve)
            tG = EMIT("scalar", lambda e, xs=x_slice: e.activation(
                s_xn[par][:], xs, AF.Identity, bias=nmr[:, 0:1], scale=rstd[:, 0:1]),
                waits=[(s_dve, tF), (s_act, tE)], inc=s_act)
            return tG

        def ffn_call(call_idx, tok_x_ready):
            nonlocal tok_wbuf, last_mm_on_buf
            g_b = {0: None, 1: None}
            rb_free = [None, None]
            xnsc_free = [None, None]
            slot_done = [None] * NT
            hg_done = [None] * NT
            offb_tok = EMIT("vector", lambda e: e.tensor_copy(
                s_offbase[:], base_sb[:]),
                waits=[(sd_c, tok_consts)], inc=s_dve)
            sc_base = [sd_sc[0].n, sd_sc[1].n]

            for t in range(NT):
                par = t % 2
                rbank = PF[0] if par == 0 else PF[3]
                xs = x_res[:, t * D:(t + 1) * D]
                tG = emit_ln(xs, [tok_x_ready[t]], par)
                tCast = EMIT("gpsimd", lambda e, par=par: e.tensor_copy(
                    s_xnbf[par][:], s_xn[par][:]),
                    waits=[(s_act, tG)], inc=s_pool)
                evs = []
                for c in range(KC):
                    bank = PB[c % 2]
                    w = [(s_pool, tCast)]
                    if len(evs) >= 2:
                        w.append((s_act, evs[-2]))
                    elif g_b.get(c % 2) is not None:
                        w.append((s_act, g_b[c % 2]))
                    tt = EMIT("tensor", lambda e, c=c, bank=bank, par=par:
                              e.transpose(
                        bank[:, 0:128], s_xnbf[par][:, c * 128:(c + 1) * 128],
                        identb_sb[:]), waits=w, inc=s_pe)
                    ev = EMIT("scalar", lambda e, c=c, bank=bank, par=par: e.copy(
                        s_lnT_bf[par][:, c * 128:(c + 1) * 128], bank[:, 0:128]),
                        waits=[(s_pe, tt)], inc=s_act)
                    evs.append(ev)
                g_b[0], g_b[1] = evs[-2], evs[-1]
                for c in range(KC):
                    w = [(s_act, evs[-1])] if c == 0 else []
                    if c == 0 and rb_free[par] is not None:
                        w.append((s_dve, rb_free[par]))
                    EMIT("tensor", lambda e, c=c, rbank=rbank, par=par: e.matmul(
                        rbank[:, 0:8], s_lnT_bf[par][:, c * 128:(c + 1) * 128],
                        wr_bf[:, c * E:(c + 1) * E],
                        start=(c == 0), stop=False, skip_group_check=True),
                        waits=w + ([(sd_wc, tok_wcast)] if c == 0 else []), inc=None)
                tR = EMIT("tensor", lambda e, rbank=rbank: e.matmul(
                    rbank[:, 0:8], onesrow_bf[0:1, :], rb_bf[0:1, :],
                    start=False, stop=True, skip_group_check=True),
                    waits=[(sd_c, tok_consts)], inc=s_pe)
                tH = EMIT("vector", lambda e, rbank=rbank, par=par: e.reduce_max(
                    s_lmaxn[par][:], rbank[:, 0:8], AX.X, negate=True),
                    waits=[(s_pe, tR)], inc=s_dve)
                tI = EMIT("scalar", lambda e, rbank=rbank, par=par: e.activation(
                    s_pe8[par][:], rbank[:, 0:8], AF.Exp, bias=s_lmaxn[par][:, 0:1],
                    accum_out=s_S[par][:, 0:1]),
                    waits=[(s_dve, tH)], inc=s_act)
                tJ = EMIT("vector", lambda e, par=par: e.reciprocal(
                    s_g[par][:], s_S[par][:]),
                    waits=[(s_act, tI)], inc=s_dve)
                tK = EMIT("vector", lambda e, rbank=rbank, par=par: e.tensor_scalar(
                    s_tmp8[par][:], rbank[:, 0:8], s_lmaxn[par][:, 0:1], 0.0,
                    ALU.add, ALU.is_ge),
                    waits=[(s_dve, tH)], inc=s_dve)
                ti0 = EMIT("vector", lambda e, par=par: e.tensor_tensor(
                    s_t8a[par][:], s_tmp8[par][:], iota8_sb[:], ALU.mult),
                    waits=[(s_dve, tK), (sd_c, tok_consts)], inc=s_dve)
                tIdx = EMIT("vector", lambda e, par=par: e.reduce_sum(
                    s_idx[par][:], s_t8a[par][:], AX.X),
                    waits=[(s_dve, ti0)], inc=s_dve)
                tHg = EMIT("vector", lambda e, par=par, t=t: e.tensor_scalar(
                    s_hg_all[:, t:t + 1], s_g[par][:, 0:1], 0.5, None, ALU.mult),
                    waits=[(s_dve, tJ)], inc=s_dve)
                hg_done[t] = tHg
                tPc2 = EMIT("gpsimd", lambda e, par=par: e.tensor_copy(
                    s_indbf[par][:], s_tmp8[par][:]),
                    waits=[(s_dve, tK)], inc=s_pool)
                w = [(s_pool, tPc2), (s_dve, tK), (s_dve, offb_tok)]
                EMIT("tensor", lambda e, rbank=rbank, par=par: e.matmul(
                    rbank[:, 8:16], trilS_sb[:], s_indbf[par][:],
                    start=True, stop=False, skip_group_check=True),
                    waits=w, inc=None)
                EMIT("tensor", lambda e, rbank=rbank, par=par: e.matmul(
                    rbank[0:1, 16:24], onescol_sb[:, 0:1], s_indbf[par][:],
                    start=False, stop=False, skip_group_check=True),
                    waits=[], inc=None)
                tRank = EMIT("tensor", lambda e, rbank=rbank: e.matmul(
                    rbank[:, 24:32], onesrow_sb[0:1, :], s_offbase[0:1, :],
                    start=True, stop=True, skip_group_check=True),
                    waits=[], inc=s_pe)
                tCn = EMIT("vector", lambda e, rbank=rbank: e.tensor_copy(
                    s_cnt[0:1, :], rbank[0:1, 16:24]),
                    waits=[(s_pe, tRank)], inc=s_dve)
                offb_tok = EMIT("vector", lambda e: e.tensor_tensor(
                    s_offbase[0:1, :], s_offbase[0:1, :], s_cnt[0:1, :], ALU.add),
                    waits=[(s_dve, tCn)], inc=s_dve)
                t1 = EMIT("vector", lambda e, rbank=rbank, par=par: e.tensor_tensor(
                    s_t8b[par][:], s_tmp8[par][:], rbank[:, 8:16], ALU.mult),
                    inc=s_dve)
                t2 = EMIT("vector", lambda e, par=par: e.reduce_sum(
                    s_slotf[par][:], s_t8b[par][:], AX.X),
                    waits=[(s_dve, t1)], inc=s_dve)
                t3 = EMIT("vector", lambda e, rbank=rbank, par=par: e.tensor_tensor(
                    s_t8b[par][:], s_tmp8[par][:], rbank[:, 24:32], ALU.mult),
                    waits=[(s_dve, t2)], inc=s_dve)
                t4 = EMIT("vector", lambda e, par=par: e.reduce_sum(
                    s_u[par][:], s_t8b[par][:], AX.X),
                    waits=[(s_dve, t3)], inc=s_dve)
                tSl = EMIT("vector", lambda e, par=par: e.tensor_tensor(
                    s_slotf[par][:], s_slotf[par][:], s_u[par][:], ALU.add),
                    waits=[(s_dve, t4)], inc=s_dve)
                rb_free[par] = tSl
                t5 = EMIT("vector", lambda e, par=par: e.scalar_tensor_tensor(
                    s_u[par][:], s_idx[par][:], -float(CAP), s_slotf[par][:],
                    ALU.mult, ALU.add),
                    waits=[(s_dve, tIdx), (s_dve, tSl)], inc=s_dve)
                t6 = EMIT("vector", lambda e, par=par: e.tensor_scalar(
                    s_o[par][:], s_u[par][:], -float(CAP), 0.0, ALU.add, ALU.is_ge),
                    waits=[(s_dve, t5)], inc=s_dve)
                t7 = EMIT("vector", lambda e, par=par: e.tensor_scalar(
                    s_u[par][:], s_slotf[par][:], -1.0, float(E * CAP),
                    ALU.mult, ALU.add), waits=[(s_dve, t6)], inc=s_dve)
                t8 = EMIT("vector", lambda e, par=par: e.tensor_tensor(
                    s_u[par][:], s_u[par][:], s_o[par][:], ALU.mult),
                    waits=[(s_dve, t7)], inc=s_dve)
                tSlf = EMIT("vector", lambda e, par=par: e.tensor_tensor(
                    s_slotf[par][:], s_slotf[par][:], s_u[par][:], ALU.add),
                    waits=[(s_dve, t8)], inc=s_dve)
                EMIT("gpsimd", lambda e, par=par, t=t: e.tensor_copy(
                    s_sloti_all[:, t:t + 1], s_slotf[par][:]),
                    waits=[(s_dve, tSlf)], inc=None)
                tInts = EMIT("gpsimd", lambda e, par=par, t=t: e.tensor_copy(
                    s_idxi_all[:, t:t + 1], s_idx[par][:]), inc=s_pool)
                slot_done[t] = tInts
                w = [(s_act, tG), (s_dve, tHg)]
                if xnsc_free[par] is not None:
                    w.append((sd_sc[par], xnsc_free[par]))
                tXs = EMIT("vector", lambda e, par=par, t=t: e.tensor_scalar(
                    s_xnsc[par][:], s_xn[par][:], s_hg_all[:, t:t + 1], None,
                    ALU.mult), waits=w, inc=s_dve)
                EMIT("gpsimd", lambda e, par=par, t=t: e.indirect_dma_start(
                    g_x[:], IndirectOffsetOnAxis(s_sloti_all[:, t:t + 1], 0),
                    s_xnsc[par][:], None),
                    waits=[(s_pool, tInts), (s_dve, tXs)], inc=sd_sc[par], k=16)
                xnsc_free[par] = sd_sc[par].n

            tok_scat = (sd_sc[0].n, sd_sc[1].n)

            gy_base = [sd_gy[0].n, sd_gy[1].n]
            gx_read = [None, None]
            gx_buf_free = [None, None]
            gxT_free = [None, None]
            ygr_free = [None, None]
            y_evac = [None, None]
            first_p2 = True
            for e_idx in range(E):
                wb = e_idx % 2
                for j in range(3):
                    g = e_idx * 3 + j
                    gp = g % 2
                    banks = (PF[1], PF[2]) if gp == 0 else (PF[4], PF[5])
                    w = []
                    if first_p2:
                        w += [(sd_sc[0], tok_scat[0]), (sd_sc[1], tok_scat[1])]
                    if gx_buf_free[gp] is not None:
                        w.append((s_pe, gx_buf_free[gp]))
                    gx_read[gp] = EMIT("sync", lambda e, e_idx=e_idx, j=j, gp=gp:
                                       e.dma_start(
                        s_gx[gp][:],
                        g_x[e_idx * CAP + j * 128: e_idx * CAP + (j + 1) * 128, :]),
                        waits=w, inc=sd_gx[gp], k=16)
                    evs = []
                    for c in range(KC):
                        bank = PB[c % 2]
                        w = [(sd_gx[gp], gx_read[gp])]
                        if first_p2 and c < 2:
                            w += [(sd_sc[0], tok_scat[0]), (sd_sc[1], tok_scat[1])]
                        if len(evs) >= 2:
                            w.append((s_act, evs[-2]))
                        elif g_b.get(c % 2) is not None:
                            w.append((s_act, g_b[c % 2]))
                        tt = EMIT("tensor", lambda e, c=c, bank=bank, gp=gp:
                                  e.transpose(
                            bank[:, 0:128], s_gx[gp][:, c * 128:(c + 1) * 128],
                            identb_sb[:]), waits=w, inc=s_pe)
                        ev = EMIT("scalar", lambda e, c=c, bank=bank, gp=gp: e.copy(
                            s_gxT[gp][:, c * 128:(c + 1) * 128], bank[:, 0:128]),
                            waits=[(s_pe, tt)], inc=s_act)
                        evs.append(ev)
                    g_b[0], g_b[1] = evs[-2], evs[-1]
                    first_p2 = False
                    gx_buf_free[gp] = None
                    mm_tok = None
                    for kc in range(KC):
                        for h in range(2):
                            w = []
                            if kc == 0 and h == 0:
                                w = [(sd_wbuf[wb], tok_wbuf[wb]),
                                     (s_act, evs[-1])]
                                if y_evac[gp] is not None:
                                    w.append((s_act, y_evac[gp]))
                            mm_tok = EMIT("tensor", lambda e, kc=kc, h=h,
                                          banks=banks, wb=wb, gp=gp: e.matmul(
                                banks[h][:, 0:512],
                                s_gxT[gp][:, kc * 128:(kc + 1) * 128],
                                w_bufs[wb][:, kc * D + h * 512: kc * D + (h + 1) * 512],
                                start=(kc == 0), stop=(kc == KC - 1),
                                skip_group_check=True),
                                waits=w,
                                inc=(s_pe if (kc == KC - 1 and h == 1) else None))
                    gx_buf_free[gp] = mm_tok
                    gxT_free[gp] = mm_tok
                    w = [(s_pe, mm_tok)]
                    if ygr_free[gp] is not None:
                        w.append((sd_gy[gp], ygr_free[gp]))
                    EMIT("scalar", lambda e, banks=banks, gp=gp: e.copy(
                        s_ygr[gp][:, 0:512], banks[0][:, 0:512]),
                        waits=w, inc=None)
                    tEv = EMIT("scalar", lambda e, banks=banks, gp=gp: e.copy(
                        s_ygr[gp][:, 512:1024], banks[1][:, 0:512]), inc=s_act)
                    y_evac[gp] = tEv
                    EMIT("sync", lambda e, e_idx=e_idx, j=j, gp=gp: e.dma_start(
                        g_y[e_idx * CAP + j * 128: e_idx * CAP + (j + 1) * 128, :],
                        s_ygr[gp][:]),
                        waits=[(s_act, tEv)], inc=sd_gy[gp], k=16)
                    ygr_free[gp] = sd_gy[gp].n
                last_mm_on_buf[wb] = mm_tok
                if e_idx + 2 <= E - 1:
                    tok_wbuf[wb] = stream_expert(e_idx + 2, wb, last_mm_on_buf[wb])

            tok_gy = (sd_gy[0].n, sd_gy[1].n)

            fin_tok = [None] * NT
            yg_free = [None, None]
            bg_free = [None, None]
            for t in range(NT):
                par = t % 2
                w = [(s_pool, slot_done[t])]
                if bg_free[par] is not None:
                    w.append((s_dve, bg_free[par]))
                EMIT("gpsimd", lambda e, par=par, t=t: e.indirect_dma_start(
                    s_bgt[par][:], None,
                    eb_bfd[:], IndirectOffsetOnAxis(s_idxi_all[:, t:t + 1], 0)),
                    waits=w + [(sd_wc, tok_wcast)], inc=sd_bg[par], k=16)
                tok_bg = sd_bg[par].n
                w = [(sd_gy[0], tok_gy[0]), (sd_gy[1], tok_gy[1])]
                if yg_free[par] is not None:
                    w.append((s_dve, yg_free[par]))
                EMIT("gpsimd", lambda e, par=par, t=t: e.indirect_dma_start(
                    s_yg[par][:], None,
                    g_y[:], IndirectOffsetOnAxis(s_sloti_all[:, t:t + 1], 0)),
                    waits=w, inc=sd_yg[par], k=16)
                tok_yg = sd_yg[par].n
                xs = x_res[:, t * D:(t + 1) * D]
                tA1 = EMIT("vector", lambda e, xs=xs, par=par: e.tensor_tensor(
                    xs, xs, s_yg[par][:], ALU.add),
                    waits=[(sd_yg[par], tok_yg), (s_dve, hg_done[t])], inc=s_dve)
                yg_free[par] = tA1
                tA2 = EMIT("vector", lambda e, xs=xs, par=par, t=t:
                           e.scalar_tensor_tensor(
                    xs, s_bgt[par][:], s_hg_all[:, t:t + 1], xs, ALU.mult, ALU.add),
                    waits=[(sd_bg[par], tok_bg)], inc=s_dve)
                bg_free[par] = tA2
                fin_tok[t] = tA2
            return fin_tok

        def prefetch_next_call_weights():
            tok_wbuf[0] = stream_expert(0, 0, last_mm_on_buf[0])
            tok_wbuf[1] = stream_expert(1, 1, last_mm_on_buf[1])

        def rope_ops(eng, sem_e, src0, dst0, t, par, waits):
            cs = cos_sb[:, t * 64:(t + 1) * 64]
            sn = sin_sb[:, t * 64:(t + 1) * 64]
            se, so = src0[:, 0:64], src0[:, 64:128]
            de, do = dst0[:, 0:64], dst0[:, 64:128]
            base = 0 if eng == "vector" else 2
            tA, tB = s_rt[base][par], s_rt[base + 1][par]
            w = list(waits) + [(sd_c, tok_consts)]
            EMIT(eng, lambda e: e.tensor_tensor(tA[:], se, cs, ALU.mult), waits=w)
            EMIT(eng, lambda e: e.tensor_tensor(tB[:], so, sn, ALU.mult))
            EMIT(eng, lambda e: e.tensor_tensor(de, tA[:], tB[:], ALU.subtract))
            EMIT(eng, lambda e: e.tensor_tensor(tA[:], se, sn, ALU.mult))
            EMIT(eng, lambda e: e.tensor_tensor(tB[:], so, cs, ALU.mult))
            return EMIT(eng, lambda e: e.tensor_tensor(do, tA[:], tB[:], ALU.add),
                        inc=sem_e)

        def attn_phase(tok_x_ready):
            tok_K = [None] * NT
            tok_V = [None] * NT
            tok_fin = [None] * NT
            g_b_evac = {0: None, 1: None}
            g_exp = {0: None, 1: None}
            g_av = {0: None, 1: None}
            g_X_free = {0: None, 1: None}
            g_wo_free = None
            prep = {}

            def emit_prep(t):
                par = t % 2
                Xb = PF[0] if par == 0 else PF[3]
                xs = x_res[:, t * D:(t + 1) * D]
                tG = emit_ln(xs, [tok_x_ready[t]], par)
                tCast = EMIT("gpsimd", lambda e, par=par: e.tensor_copy(
                    s_xnbf[par][:], s_xn[par][:]),
                    waits=[(s_act, tG)], inc=s_pool)
                evs = []
                for c in range(KC):
                    bank = PB[c % 2]
                    w = [(s_pool, tCast)]
                    if t == 0 and c < 2:
                        w.append((s_act, tok_setup_tr))
                    if len(evs) >= 2:
                        w.append((s_act, evs[-2]))
                    elif g_b_evac.get(c % 2) is not None:
                        w.append((s_act, g_b_evac[c % 2]))
                    tt = EMIT("tensor", lambda e, c=c, bank=bank, par=par:
                              e.transpose(
                        bank[:, 0:128], s_xnbf[par][:, c * 128:(c + 1) * 128],
                        identb_sb[:]), waits=w, inc=s_pe)
                    ev = EMIT("scalar", lambda e, c=c, bank=bank, par=par: e.copy(
                        s_lnT_bf[par][:, c * 128:(c + 1) * 128], bank[:, 0:128]),
                        waits=[(s_pe, tt)], inc=s_act)
                    evs.append(ev)
                mm_tok = None
                for kc in range(KC):
                    w = []
                    if kc == 0:
                        w = [(s_act, evs[-1])]
                        xf = g_X_free.get(par)
                        if xf is not None:
                            w.append((s_act, xf[0]))
                            w.append((s_dve, xf[1]))
                    mm_tok = EMIT("tensor", lambda e, kc=kc, par=par, Xb=Xb:
                                  e.matmul(
                        Xb[:, 0:512],
                        s_lnT_bf[par][:, kc * 128:(kc + 1) * 128],
                        qkvT[:, kc * 512:(kc + 1) * 512],
                        start=(kc == 0), stop=(kc == KC - 1),
                        skip_group_check=True),
                        waits=w, inc=(s_pe if kc == KC - 1 else None))
                tQK = mm_tok
                ev_q0 = EMIT("scalar", lambda e, par=par, Xb=Xb: e.copy(
                    s_qk[par][:, 0:128], Xb[:, 0:128]),
                    waits=[(s_pe, tQK)], inc=s_act)
                ev_q1 = EMIT("scalar", lambda e, par=par, Xb=Xb: e.copy(
                    s_qk[par][:, 128:256], Xb[:, 128:256]), inc=s_act)
                ev_k = EMIT("scalar", lambda e, par=par, Xb=Xb: e.copy(
                    s_qk[par][:, 256:384], Xb[:, 256:384]), inc=s_act)
                ev_v = EMIT("scalar", lambda e, t=t, Xb=Xb: e.copy(
                    V_tok[:, t * 128:(t + 1) * 128], Xb[:, 384:512]), inc=s_act)
                tok_V[t] = ev_v
                r_q0 = rope_ops("vector", s_dve, s_qk[par][:, 0:128],
                                s_qkr[par][:, 0:128], t, par, [(s_act, ev_q0)])
                r_k = rope_ops("vector", s_dve, s_qk[par][:, 256:384],
                               s_qkr[par][:, 256:384], t, par, [(s_act, ev_k)])
                r_q1 = rope_ops("gpsimd", s_pool, s_qk[par][:, 128:256],
                                s_qkr[par][:, 128:256], t, par, [(s_act, ev_q1)])
                ev_tr0 = None
                tr_specs = [
                    (s_qkr[par][:, 0:128], (lambda par=par: s_qT[par][:, 0:128]),
                     (s_dve, r_q0)),
                    (s_qkr[par][:, 128:256], (lambda par=par: s_qT[par][:, 128:256]),
                     (s_pool, r_q1)),
                    (s_qkr[par][:, 256:384],
                     (lambda t=t: K_hs[:, t * 128:(t + 1) * 128]), (s_dve, r_k)),
                ]
                for i, (srcp, dstf, wtok) in enumerate(tr_specs):
                    bank = PB[i % 2]
                    w = [wtok, (s_act, evs[-2 + (i % 2)])]
                    if i >= 2:
                        w.append((s_act, ev_tr0))
                    tt = EMIT("tensor", lambda e, srcp=srcp, bank=bank: e.transpose(
                        bank[:, 128:256], srcp, identb_sb[:]),
                        waits=w, inc=s_pe)
                    ev = EMIT("scalar", lambda e, dstf=dstf, bank=bank: e.copy(
                        dstf(), bank[:, 128:256]),
                        waits=[(s_pe, tt)], inc=s_act)
                    if i == 0:
                        ev_tr0 = ev
                    g_b_evac[i % 2] = ev
                tok_K[t] = ev
                prep[t] = dict(qT=ev, v=ev_v, qkv_last_evac=ev_v)

            def emit_main(t):
                nonlocal g_wo_free
                par = t % 2
                Xb = PF[0] if par == 0 else PF[3]
                tok_qT = prep[t]["qT"]
                ev_v = prep[t]["v"]
                tAV = None

                def emit_sc(kt):
                    sbk = 1 + (kt % 2)
                    w = [(s_act, tok_K[kt]), (s_act, tok_qT)]
                    if g_exp.get(sbk - 1) is not None:
                        w.append((s_act, g_exp[sbk - 1]))
                    return EMIT("tensor", lambda e, kt=kt, sbk=sbk, par=par:
                                e.matmul(
                        PF[sbk][:, 0:256], K_hs[:, kt * 128:(kt + 1) * 128],
                        s_qT[par][:, 0:256], start=True, stop=True,
                        skip_group_check=True),
                        waits=w, inc=s_pe)

                tSc_next = emit_sc(0)
                for kt in range(t + 1):
                    sbk = 1 + (kt % 2)
                    slot = kt % 2
                    tSc = tSc_next
                    wx = [(s_pe, tSc), (s_dve, tok_negC)]
                    if g_av.get(slot) is not None:
                        wx.append((s_pe, g_av[slot]))
                    tEx = EMIT("scalar", lambda e, sbk=sbk, slot=slot: e.activation(
                        s_P[:, slot * 256:(slot + 1) * 256], PF[sbk][:, 0:256],
                        AF.Exp, bias=s_negC[:, 0:1], scale=ATTN_SCALE),
                        waits=wx, inc=s_act)
                    g_exp[sbk - 1] = tEx
                    if kt + 1 <= t:
                        tSc_next = emit_sc(kt + 1)
                    gate_tok = (s_act, tEx)
                    if kt == t:
                        EMIT("gpsimd", lambda e, slot=slot: e.tensor_tensor(
                            s_P[:, slot * 256: slot * 256 + 128],
                            s_P[:, slot * 256: slot * 256 + 128], tril_sb[:],
                            ALU.mult),
                            waits=[(s_act, tEx), (sd_c, tok_consts)], inc=None)
                        tMk = EMIT("gpsimd", lambda e, slot=slot: e.tensor_tensor(
                            s_P[:, slot * 256 + 128: slot * 256 + 256],
                            s_P[:, slot * 256 + 128: slot * 256 + 256], tril_sb[:],
                            ALU.mult), waits=[], inc=s_pool)
                        gate_tok = (s_pool, tMk)
                    w = [gate_tok, (s_act, tok_V[kt] if kt < t else ev_v)]
                    if kt == 0:
                        w.append((s_act, prep[t]["qkv_last_evac"]))
                        w.append((sd_c, tok_consts))
                    EMIT("tensor", lambda e, kt=kt, slot=slot, t=t, Xb=Xb: e.matmul(
                        Xb[:, 0:256], V_tok[:, kt * 128:(kt + 1) * 128],
                        s_P[:, slot * 256:(slot + 1) * 256],
                        start=(kt == 0), stop=(kt == t),
                        skip_group_check=True),
                        waits=w, inc=None)
                    tAV = EMIT("tensor", lambda e, kt=kt, slot=slot, t=t, Xb=Xb:
                               e.matmul(
                        Xb[0:1, 256:512], onescol_sb[:, 0:1],
                        s_P[:, slot * 256:(slot + 1) * 256],
                        start=False, stop=(kt == t), skip_group_check=True),
                        waits=[], inc=s_pe)
                    g_av[slot] = tAV
                tOT = EMIT("scalar", lambda e, par=par, Xb=Xb: e.copy(
                    s_oT[par][:], Xb[:, 0:256]),
                    waits=[(s_pe, tAV)], inc=s_act)
                tLe = EMIT("scalar", lambda e, par=par, Xb=Xb: e.copy(
                    s_lrow[par][0:1, :], Xb[0:1, 256:512]), inc=s_act)
                g_X_free[par] = (tLe, None)
                tRI = EMIT("vector", lambda e, par=par: e.reciprocal(
                    s_rinv[par][0:1, :], s_lrow[par][0:1, :]),
                    waits=[(s_act, tLe)], inc=s_dve)
                w = [(s_dve, tRI), (sd_c, tok_consts)]
                if g_wo_free is not None:
                    w.append((s_dve, g_wo_free))
                tBC = EMIT("tensor", lambda e, par=par: e.matmul(
                    PF[4][:, 0:256], onesrow_sb[0:1, 0:128], s_rinv[par][0:1, 0:256],
                    start=True, stop=True, skip_group_check=True),
                    waits=w, inc=s_pe)
                tOS = EMIT("vector", lambda e, par=par: e.tensor_tensor(
                    s_oT[par][:], s_oT[par][:], PF[4][:, 0:256], ALU.mult),
                    waits=[(s_act, tOT), (s_pe, tBC)], inc=s_dve)
                wo_tok = None
                for half in range(2):
                    for h in range(2):
                        w = [(s_dve, tOS)] if (half == 0 and h == 0) else []
                        wo_tok = EMIT("tensor", lambda e, half=half, h=h, par=par:
                                      e.matmul(
                            PF[4 + half][:, 0:512],
                            s_oT[par][:, h * 128:(h + 1) * 128],
                            woT[:, h * D + half * 512: h * D + (half + 1) * 512],
                            start=(h == 0), stop=(h == 1), skip_group_check=True),
                            waits=w,
                            inc=(s_pe if (half == 1 and h == 1) else None))
                tFin = None
                for half in range(2):
                    xs = x_res[:, t * D + half * 512: t * D + (half + 1) * 512]
                    tFin = EMIT("vector", lambda e, xs=xs, half=half: e.tensor_tensor(
                        xs, xs, PF[4 + half][:, 0:512], ALU.add),
                        waits=([(s_pe, wo_tok)] if half == 0 else []),
                        inc=(s_dve if half == 1 else None))
                g_wo_free = tFin
                tok_fin[t] = tFin

            if ATTN_PIPE:
                emit_prep(0)
                for t in range(NT):
                    if t + 1 < NT:
                        emit_prep(t + 1)
                    emit_main(t)
            else:
                for t in range(NT):
                    emit_prep(t)
                    emit_main(t)
            return tok_fin


        n_ffn_calls = (("ffn1" in PHASES) + ("ffn2" in PHASES)) * SEQ
        calls_done = 0

        for s in range(SEQ):
            x_ready = [(sd_x, tok_xload_all) for _ in range(NT)]

            if "ffn1" in PHASES:
                fin1 = ffn_call(0, x_ready)
                x_ready = [(s_dve, fin1[t]) for t in range(NT)]
                calls_done += 1
                if calls_done < n_ffn_calls:
                    prefetch_next_call_weights()

            if "attn" in PHASES:
                fin_a = attn_phase(x_ready)
                x_ready = [(s_dve, fin_a[t]) for t in range(NT)]

            if "ffn2" in PHASES:
                fin2 = ffn_call(1, x_ready)
                x_ready = [(s_dve, fin2[t]) for t in range(NT)]
                calls_done += 1
                if calls_done < n_ffn_calls:
                    prefetch_next_call_weights()

            for t in range(NT):
                EMIT("sync", lambda e, s=s, t=t: e.dma_start(
                    y_out[s, t * 128:(t + 1) * 128, :], x_res[:, t * D:(t + 1) * D]),
                    waits=[x_ready[t]], inc=sd_st, k=16)
            tok_store_all = sd_st.n
            if s + 1 < SEQ:
                for t in range(NT):
                    EMIT("sync", lambda e, s=s, t=t: e.dma_start(
                        x_res[:, t * D:(t + 1) * D],
                        x_in[s + 1, t * 128:(t + 1) * 128, :]),
                        waits=([(sd_st, tok_store_all)] if t == 0 else []),
                        inc=sd_x, k=16)
                tok_xload_all = sd_x.n

        EMIT("sync", lambda e: e.nop(), waits=[(sd_st, sd_st.n)])

        with nc.Block() as block:
            for eng_name in KB.ENGINES:
                closures = kb.q[eng_name]
                getattr(block, eng_name)(
                    lambda e, closures=closures: [f(e) for f in closures])

    return nc


_NC_CACHE = None


def _get_nc():
    global _NC_CACHE
    if _NC_CACHE is None:
        _NC_CACHE = build_program()
    return _NC_CACHE


def _rope_tables():
    positions = np.arange(0, HS, 2, dtype=np.float32)
    theta = (10000.0 ** (-2.0 * positions / HS)).astype(np.float32)
    ang = np.arange(T, dtype=np.float32)[:, None] * theta[None, :]
    cos = np.cos(ang).astype(np.float32)
    sin = np.sin(ang).astype(np.float32)
    cos = cos.reshape(NT, 128, 64).transpose(1, 0, 2).reshape(128, NT * 64)
    sin = sin.reshape(NT, 128, 64).transpose(1, 0, 2).reshape(128, NT * 64)
    return cos.astype(ml_dtypes.bfloat16), sin.astype(ml_dtypes.bfloat16)


def kernel(L0_attn_x, L0_attn_Wq, L0_attn_Wk, L0_attn_Wv, L0_attn_Wo,
           L0_ffn_router_w, L0_ffn_router_b, L0_ffn_expert_w, L0_ffn_expert_b):
    x = np.ascontiguousarray(np.asarray(L0_attn_x, dtype=np.float32))
    perm = np.concatenate([np.arange(0, HS, 2), np.arange(1, HS, 2)])
    wq_p = np.ascontiguousarray(np.asarray(L0_attn_Wq, np.float32)[:, perm, :])
    wk_p = np.ascontiguousarray(np.asarray(L0_attn_Wk, np.float32)[perm, :])
    wv = np.ascontiguousarray(np.asarray(L0_attn_Wv, np.float32))
    wo = np.ascontiguousarray(np.asarray(L0_attn_Wo, np.float32))
    wr = np.ascontiguousarray(np.asarray(L0_ffn_router_w, np.float32))
    rbv = np.ascontiguousarray(np.asarray(L0_ffn_router_b, np.float32)).reshape(1, E)
    ewv = np.ascontiguousarray(np.asarray(L0_ffn_expert_w, np.float32))
    ebv = np.ascontiguousarray(np.asarray(L0_ffn_expert_b, np.float32))

    cosb, sinb = _rope_tables()
    identf = np.eye(128, dtype=np.float32)
    identb = np.eye(128, dtype=np.float32).astype(ml_dtypes.bfloat16)
    kq = np.arange(128)
    tril = (kq[None, :] >= kq[:, None]).astype(np.float32).astype(ml_dtypes.bfloat16)
    ones_col = np.ones((128, 1), np.float32).astype(ml_dtypes.bfloat16)
    ones_row = np.ones((1, 128), np.float32)
    ones_row_bf = np.ones((1, 128), np.float32).astype(ml_dtypes.bfloat16)
    trilS = (kq[None, :] > kq[:, None]).astype(np.float32).astype(ml_dtypes.bfloat16)
    iota8 = np.broadcast_to(np.arange(E, dtype=np.float32), (128, E)).copy()
    base8 = (np.arange(E, dtype=np.float32) * CAP).reshape(1, E)

    nc = _get_nc()
    in_maps = []
    for c in range(NCORES):
        in_maps.append({
            "x_in": x[c * SEQ:(c + 1) * SEQ],
            "wq": wq_p, "wk": wk_p, "wv": wv, "wo": wo,
            "wr": wr, "rb": rbv, "ew": ewv, "eb": ebv,
            "c_identf": identf, "c_identb": identb, "c_tril": tril,
            "c_cos": cosb, "c_sin": sinb,
            "c_ones_col": ones_col, "c_ones_row": ones_row,
            "c_ones_row_bf": ones_row_bf, "c_trilS": trilS,
            "c_iota8": iota8, "c_base": base8,
        })
    res = run_bass_kernel_spmd(nc, in_maps, core_ids=list(range(NCORES)))
    out = np.concatenate([r["y"] for r in res.results], axis=0)
    return out.reshape(B, T, D)
